# version 25
# baseline (speedup 1.0000x reference)
import contextlib
import numpy as np

import concourse.bass as bass
import concourse.mybir as mybir
from concourse.bass_utils import run_bass_kernel_spmd

F32 = mybir.dt.float32
F16 = mybir.dt.float16
I8 = mybir.dt.int8
AF = mybir.ActivationFunctionType
ALU = mybir.AluOpType
AX = mybir.AxisListType

NCORES = 8
T = 2048
D = 1024
H = 16
NT = T // 128
NCH = D // 128
NP = H // 2
QC = T // 512

RSQRT2 = 0.7071067811865476
QSCALE = 0.125
EPS_Z = 1e-6
EPS_LN = 1e-5
MAGIC = 12582912.0
INV1024 = 1.0 / 1024.0
INV127 = 1.0 / 127.0


class Sem:
    def __init__(self, handle, name=""):
        self.h = handle
        self.n = 0
        self.name = name


class Plan:

    ENGINES = ("sync", "gpsimd", "tensor", "scalar", "vector")

    def __init__(self, nc, last_wait):
        self.nc = nc
        self.streams = {e: [] for e in self.ENGINES}
        self.last_wait = last_wait
        self.waited = {}
        self.future_waits = {}

    def op(self, stream, builder, waits=(), incs=()):
        waits = list(waits)
        for s, k in incs:
            if k == 16:
                fw = self.future_waits.get(s.name)
                if fw:
                    import bisect
                    i = bisect.bisect_right(fw, s.n)
                    if i > 0 and fw[i - 1] > 0:
                        waits.append((s, fw[i - 1]))
        wl = []
        for s, v in waits:
            if v <= 0:
                continue
            self.waited.setdefault(s.name, set()).add(v)
            key = (stream, id(s))
            if self.last_wait.get(key, -1) >= v:
                continue
            self.last_wait[key] = v
            wl.append((s, v))
        for s, k in incs:
            s.n += k

        def thunk(eng, wl=wl, builder=builder, incs=tuple(incs)):
            for s, v in wl:
                eng.wait_ge(s.h, v)
            inst = builder(eng)
            for s, k in incs:
                inst = inst.then_inc(s.h, k)

        self.streams[stream].append(thunk)

    def emit(self):
        if not any(self.streams.values()):
            return
        with self.nc.Block() as block:
            for name in self.ENGINES:
                ths = self.streams[name]
                if not ths:
                    continue
                getattr(block, name)(lambda eng, ths=ths: [t(eng) for t in ths])


def build_nc(debug=False, _waits=None):
    if _waits is None:
        _waits = _collect_waits(debug)
    return _build_nc_inner(debug, _waits)


def _collect_waits(debug):
    nc, plan = _build_nc_inner(debug, {}, collect_only=True)
    return {k: sorted(v) for k, v in plan.waited.items()}


def _build_nc_inner(debug, _waits, collect_only=False):
    nc = bass.Bass("TRN2", target_bir_lowering=False,
                   dynamic_dma_scratch_size=8192)
    if debug:
        d_qto = nc.dram_tensor("qto", [128, NP * T], F16, kind="ExternalOutput").ap()
        d_kto = nc.dram_tensor("kto", [128, NP * T], F16, kind="ExternalOutput").ap()
        d_vzo = nc.dram_tensor("vzo", [128, NT * H * 66], F16, kind="ExternalOutput").ap()
        d_oto = nc.dram_tensor("oto", [128, NCH * T], F16, kind="ExternalOutput").ap()
        d_x1o = nc.dram_tensor("x1o", [T, D], F32, kind="ExternalOutput").ap()
        d_oso = nc.dram_tensor("oso", [65, 2 * T], F32, kind="ExternalOutput").ap()
        d_xqo = nc.dram_tensor("xqo", [128, NCH * T], F16, kind="ExternalOutput").ap()
        d_r2o = nc.dram_tensor("r2o", [128, NT * D], F32, kind="ExternalOutput").ap()

    d_xtok = nc.dram_tensor("x_tok", [T, D], F32, kind="ExternalInput").ap()
    d_xT = nc.dram_tensor("xT", [D, T], F32, kind="ExternalInput").ap()
    d_wqt = nc.dram_tensor("wqt", [D, D], F32, kind="ExternalInput").ap()
    d_wkt = nc.dram_tensor("wkt", [D, D], F32, kind="ExternalInput").ap()
    d_wvt = nc.dram_tensor("wvt", [D, D], F32, kind="ExternalInput").ap()
    d_wot = nc.dram_tensor("wot", [D, D], F32, kind="ExternalInput").ap()
    d_wqq = nc.dram_tensor("wqq", [D, D], I8, kind="ExternalInput").ap()
    d_wsb = nc.dram_tensor("wsb", [128, D], F32, kind="ExternalInput").ap()
    d_biasb = nc.dram_tensor("biasb", [128, D], F32, kind="ExternalInput").ap()
    d_tri = nc.dram_tensor("tri", [128, 128], F16, kind="ExternalInput").ap()
    d_ident = nc.dram_tensor("ident", [128, 128], F16, kind="ExternalInput").ap()
    d_ones2 = nc.dram_tensor("ones2", [2, 64], F16, kind="ExternalInput").ap()
    d_out = nc.dram_tensor("out", [T, D], F32, kind="ExternalOutput").ap()
    d_rzscr = nc.dram_tensor("rzscr", [2 * T], F32, kind="Internal").ap()

    ctx = contextlib.ExitStack()
    e = ctx.enter_context
    last_wait = {}

    sCa = Sem(e(nc.semaphore("sCa")), "sCa")
    sIO = Sem(e(nc.semaphore("sIO")), "sIO")
    sPE = Sem(e(nc.semaphore("sPE")), "sPE")
    sSc = Sem(e(nc.semaphore("sSc")), "sSc")
    sDV = Sem(e(nc.semaphore("sDV")), "sDV")
    sPo = Sem(e(nc.semaphore("sPo")), "sPo")

    R = dict(side="right")
    wsb = e(nc.sbuf_tensor("wsb_sb", [128, D], F32, **R))
    biasb = e(nc.sbuf_tensor("biasb_sb", [128, D], F32, **R))
    tri = e(nc.sbuf_tensor("tri_sb", [128, 128], F16, **R))
    ident = e(nc.sbuf_tensor("ident_sb", [128, 128], F16, **R))
    ones2 = e(nc.sbuf_tensor("ones2_sb", [2, 64], F16, **R))
    consts = e(nc.sbuf_tensor("consts", [128, 4], F32, **R))
    ot_fin = e(nc.sbuf_tensor("ot_fin", [128, NCH * T], F16, **R))
    ob_tmp = e(nc.sbuf_tensor("ob_tmp", [64, T], F16, **R))
    xstream = [e(nc.sbuf_tensor(f"xs{i}", [128, D], F32, **R)) for i in range(2)]
    stat = e(nc.sbuf_tensor("stat", [128, 13 * NT], F32, **R))
    woT = e(nc.sbuf_tensor("woT", [128, NCH * D], F16, **R))

    c_rs = consts[:, 0:1]
    c_eps = consts[:, 1:2]
    c_negm = consts[:, 2:3]
    S1 = stat[:, 0:NT]
    SQ1 = stat[:, NT:2 * NT]
    M1 = stat[:, 2 * NT:3 * NT]
    V1 = stat[:, 3 * NT:4 * NT]
    MSQ = stat[:, 4 * NT:5 * NT]
    AM = stat[:, 5 * NT:6 * NT]
    XS = stat[:, 6 * NT:7 * NT]
    RQ = stat[:, 7 * NT:8 * NT]
    MN = stat[:, 8 * NT:9 * NT]
    MX = stat[:, 9 * NT:10 * NT]
    S2 = stat[:, 10 * NT:11 * NT]
    MM2 = stat[:, 11 * NT:12 * NT]
    NMR = stat[:, 12 * NT:13 * NT]

    scope_qkv = contextlib.ExitStack()
    eq = scope_qkv.enter_context
    qT = eq(nc.sbuf_tensor("qT", [128, NP * T], F16))
    kT = eq(nc.sbuf_tensor("kT", [128, NP * T], F16))
    vz = eq(nc.sbuf_tensor("vz", [128, NT * H * 66], F16))
    vz_blocks = vz.rearrange("p (b c) -> p b c", c=66)

    scope_p = contextlib.ExitStack()
    ep = scope_p.enter_context
    xTsb = ep(nc.sbuf_tensor("xTsb", [128, NCH * T], F16))
    wbuf = ep(nc.sbuf_tensor("wbuf", [128, NCH * D], F16))

    plan = Plan(nc, last_wait)
    plan.future_waits = _waits

    for dst, src in ((wsb, d_wsb), (biasb, d_biasb), (tri, d_tri),
                     (ident, d_ident), (ones2, d_ones2)):
        plan.op("sync", lambda sy, dst=dst, src=src: sy.dma_start(dst[:, :], src),
                incs=[(sIO, 16)])
    n_const = sIO.n

    plan.op("vector", lambda v: v.memset(consts[:, 0:1], RSQRT2))
    plan.op("vector", lambda v: v.memset(consts[:, 1:2], EPS_LN))
    plan.op("vector", lambda v: v.memset(consts[:, 2:3], -MAGIC), incs=[(sDV, 1)])
    n_consts_dv = sDV.n

    dma_after = {}
    for fi in range(NCH):
        plan.op("gpsimd",
                lambda g, fi=fi: g.dma_start(xTsb[:, fi * T:(fi + 1) * T],
                                             d_xT[fi * 128:(fi + 1) * 128, :]),
                incs=[(sCa, 16)])
        plan.op("gpsimd",
                lambda g, fi=fi: g.dma_start(wbuf[:, fi * D:(fi + 1) * D],
                                             d_wqt[fi * 128:(fi + 1) * 128, :]),
                incs=[(sCa, 16)])
        dma_after[(0, fi)] = sCa.n

    ph1_psum = contextlib.ExitStack()
    psq = [ph1_psum.enter_context(nc.psum_tensor(f"pq{i}", [128, T], F32))
           for i in range(2)]

    evac_dv = {}
    mm_done = {}
    w_last_use = {}
    for proj in range(2):
        for c in range(NCH):
            ps = psq[c % 2]
            prev = None
            if c >= 2:
                prev = (proj, c - 2)
            elif proj > 0:
                prev = (proj - 1, NCH - 2 + c)
            for fi in range(NCH):
                waits = [(sCa, dma_after[(proj, fi)])]
                if prev in evac_dv:
                    waits.append((sDV, evac_dv[prev]))
                for qcc in range(QC):
                    def mk(t, ps=ps, fi=fi, qcc=qcc, c=c):
                        lhs = wbuf[:, fi * D + c * 128: fi * D + c * 128 + 128]
                        rhs = xTsb[:, fi * T + qcc * 512: fi * T + qcc * 512 + 512]
                        return t.matmul(ps[:, qcc * 512:(qcc + 1) * 512], lhs, rhs,
                                        start=(fi == 0), stop=(fi == NCH - 1))
                    incs = ()
                    if qcc == QC - 1 and (c == NCH - 1 or fi == NCH - 1):
                        incs = [(sPE, 1)]
                    plan.op("tensor", mk, waits=waits if qcc == 0 else (),
                            incs=incs)
                if c == NCH - 1:
                    w_last_use[(proj, fi)] = sPE.n
            mm_done[(proj, c)] = sPE.n
            dst = qT if proj == 0 else kT
            scale = QSCALE if proj == 0 else 1.0
            def mkev(v, ps=ps, dst=dst, scale=scale, c=c):
                return v.tensor_scalar(dst[:, c * T:(c + 1) * T], ps[:, :],
                                       scale, None, ALU.mult)
            plan.op("vector", mkev, waits=[(sPE, mm_done[(proj, c)])],
                    incs=[(sDV, 1)])
            evac_dv[(proj, c)] = sDV.n
        nxt = d_wkt if proj == 0 else d_wvt
        for fi in range(NCH):
            plan.op("gpsimd",
                    lambda g, fi=fi, nxt=nxt: g.dma_start(
                        wbuf[:, fi * D:(fi + 1) * D],
                        nxt[fi * 128:(fi + 1) * 128, :]),
                    waits=[(sPE, w_last_use[(proj, fi)])],
                    incs=[(sCa, 16)])
            dma_after[(proj + 1, fi)] = sCa.n
    n_qk_evac = sDV.n
    n_w3 = sCa.n

    ph1_psum.close()

    ph1b_psum = contextlib.ExitStack()
    psv = [ph1b_psum.enter_context(nc.psum_tensor(f"pv{i}", [128, D], F32))
           for i in range(2)]

    plan.op("vector", lambda v: v.memset(vz_blocks[:, :, 64:65], 1.0),
            incs=[(sDV, 1)])

    v_evac = {}
    for t in range(NT):
        ps = psv[t % 2]
        extra = []
        if t == 0:
            extra.append((sDV, n_qk_evac))
        if t >= 2:
            extra.append((sDV, v_evac[t - 2]))
        for fi in range(NCH):
            waits = [(sCa, dma_after[(2, fi)])] + extra
            extra = []
            for half in range(2):
                def mk(tt, ps=ps, fi=fi, half=half, t=t):
                    lhs = xTsb[:, fi * T + t * 128: fi * T + t * 128 + 128]
                    rhs = wbuf[:, fi * D + half * 512: fi * D + half * 512 + 512]
                    return tt.matmul(ps[:, half * 512:(half + 1) * 512], lhs, rhs,
                                     start=(fi == 0), stop=(fi == NCH - 1))
                incs = [(sPE, 1)] if (fi == NCH - 1 and half == 1) else ()
                plan.op("tensor", mk, waits=waits if half == 0 else (), incs=incs)
                waits = ()
        n_mm = sPE.n
        def mkev(v, ps=ps, t=t):
            src = ps.rearrange("p (h d) -> p h d", d=64)
            dst = vz_blocks[:, t * H:(t + 1) * H, 0:64]
            return v.tensor_copy(dst, src)
        plan.op("vector", mkev, waits=[(sPE, n_mm)], incs=[(sDV, 1)])
        v_evac[t] = sDV.n
    n_vz_done = sDV.n

    if debug:
        plan.op("sync", lambda sy: sy.dma_start(d_qto, qT[:, :]),
                waits=[(sDV, n_qk_evac)], incs=[(sIO, 16)])
        plan.op("sync", lambda sy: sy.dma_start(d_kto, kT[:, :]),
                waits=[(sDV, n_qk_evac)], incs=[(sIO, 16)])
        plan.op("sync", lambda sy: sy.dma_start(d_vzo, vz[:, :]),
                waits=[(sDV, n_vz_done)], incs=[(sIO, 16)])

    ph1b_psum.close()
    scope_p.close()

    scope_a = contextlib.ExitStack()
    ea = scope_a.enter_context
    uab = [ea(nc.sbuf_tensor(f"uab{i}", [128, 1024], F16)) for i in range(4)]
    ot_sb = [ea(nc.sbuf_tensor(f"otsb{i}", [65, 2 * T], F32)) for i in range(2)]
    zg = ea(nc.sbuf_tensor("zg", [32, 128], F32))
    rz2 = ea(nc.sbuf_tensor("rz2", [32, 128], F32))
    rzb = ea(nc.sbuf_tensor("rzb", [64, T], F32))

    ph2_psum = contextlib.ExitStack()
    psab = [ph2_psum.enter_context(nc.psum_tensor(f"ps{i}", [128, 1024], F32))
            for i in range(2)]
    oab = [ph2_psum.enter_context(nc.psum_tensor(f"oab{i}", [128, 1024], F32))
           for i in range(2)]

    for fi in range(NCH):
        plan.op("gpsimd",
                lambda g, fi=fi: g.dma_start(
                    woT[:, fi * D:(fi + 1) * D],
                    d_wot[fi * 128:(fi + 1) * 128, :]),
                incs=[(sCa, 16)])
    n_wo = sCa.n

    s_done = {}
    sq_done = {}
    o_done = {}
    oevac_done = {}
    div_done = {}
    bcast_done = {}
    bcastB_done = {}
    gather_done = {}
    scatter_done = {}
    obmove_done = {}
    stage1_dv = {}

    def plan_div_stage1(p):
        plan.op("vector",
                lambda v: v.tensor_scalar(zg[:, :], zg[:, :], EPS_Z, None, ALU.add),
                waits=[(sIO, gather_done[p])], incs=[(sDV, 1)])
        plan.op("vector", lambda v: v.reciprocal(rz2[:, :], zg[:, :]),
                waits=[(sDV, sDV.n)], incs=[(sDV, 1)])
        stage1_dv[p] = sDV.n
        w = [(sDV, stage1_dv[p])]
        if p > 0:
            w.append((sCa, bcastB_done.get(p - 1, 0)))
        plan.op("sync", lambda sy: sy.dma_start(d_rzscr, rz2[:, :]),
                waits=w, incs=[(sIO, 16)])
        scatter_done[p] = sIO.n
        wb = [(sIO, scatter_done[p])]
        if p > 0:
            wb.append((sDV, div_done.get(p - 1, 0)))
        def mkb(g):
            return g.dma_start(rzb[:, :],
                               d_rzscr[0:T][None, :].broadcast_to([64, T]))
        plan.op("gpsimd", mkb, waits=wb, incs=[(sCa, 16)])
        bcast_done[p] = sCa.n

    def plan_div_stage2a(p):
        c = p
        wv = [(sCa, bcast_done[p]), oevac_done[p * QC + QC - 1]]
        for qc in range(QC):
            def mka(v, p=p, qc=qc, c=c):
                return v.tensor_tensor(
                    ot_fin[0:64, c * T + qc * 512: c * T + qc * 512 + 512],
                    ot_sb[p % 2][0:64, qc * 512: qc * 512 + 512],
                    rzb[:, qc * 512: qc * 512 + 512], ALU.mult)
            plan.op("vector", mka, waits=wv, incs=[(sDV, 1)])
            wv = ()
        nda = sDV.n
        def mkbB(g):
            return g.dma_start(rzb[:, :],
                               d_rzscr[T:2 * T][None, :].broadcast_to([64, T]))
        plan.op("gpsimd", mkbB, waits=[(sDV, nda)], incs=[(sCa, 16)])
        bcastB_done[p] = sCa.n

    def plan_div_stage2b(p):
        wv = [(sCa, bcastB_done[p])]
        if p > 0:
            wv.append((sIO, obmove_done.get(p - 1, 0)))
        for qc in range(QC):
            def mkb2(v, p=p, qc=qc):
                return v.tensor_tensor(
                    ob_tmp[0:64, qc * 512: qc * 512 + 512],
                    ot_sb[p % 2][0:64, T + qc * 512: T + qc * 512 + 512],
                    rzb[:, qc * 512: qc * 512 + 512], ALU.mult)
            plan.op("vector", mkb2, waits=wv, incs=[(sDV, 1)])
            wv = ()
        div_done[p] = sDV.n
        wmv = [(sDV, div_done[p])]
        if p > 0:
            wmv.append((sIO, obmove_done.get(p - 1, 0)))
        c = p
        def mkmv(sy, c=c):
            return sy.dma_start(ot_fin[64:128, c * T:(c + 1) * T], ob_tmp[0:64, :])
        plan.op("sync", mkmv, waits=wmv, incs=[(sIO, 16)])
        obmove_done[p] = sIO.n

    LOOK = 3

    gk = 0
    aff_done = {}
    for p in range(NP):
        qT_c = qT[:, p * T:(p + 1) * T]
        kT_c = kT[:, p * T:(p + 1) * T]

        tiles = []
        for qc in range(QC):
            for kt in range(4 * qc + 4):
                tiles.append((qc, kt, gk))
                gk += 1
        ntl = len(tiles)
        base_gk = tiles[0][2]

        def emit_s(idx, p=p, tiles=tiles, qT_c=qT_c, kT_c=kT_c):
            qc, kt, g = tiles[idx]
            j = kt - 4 * qc
            js = 128 * j if j >= 0 else 0
            ps = psab[g % 2]
            wpe = []
            if g >= 2:
                wpe.append((sSc, sq_done[g - 2]))
            if g == 0:
                wpe.append((sDV, n_vz_done))
            for ab in range(2):
                r0 = ab * 64
                def mks(t, ab=ab, r0=r0, kt=kt, qc=qc, js=js, ps=ps,
                        qT_c=qT_c, kT_c=kT_c):
                    lhs = kT_c[r0:r0 + 64, kt * 128:(kt + 1) * 128]
                    rhs = qT_c[r0:r0 + 64, qc * 512 + js: qc * 512 + 512]
                    return t.matmul(ps[:, ab * 512 + js: ab * 512 + 512],
                                    lhs, rhs, start=True, stop=True,
                                    tile_position=(r0, 0))
                plan.op("tensor", mks, waits=wpe if ab == 0 else (),
                        incs=[(sPE, 1)] if ab == 1 else ())
                wpe = ()
            s_done[g] = sPE.n

        def emit_sq_aff(idx, tiles=tiles):
            qc, kt, g = tiles[idx]
            j = kt - 4 * qc
            js = 128 * j if j >= 0 else 0
            ps = psab[g % 2]
            ubuf = uab[g % 4]
            wsc = [(sPE, s_done[g]), (sDV, n_consts_dv)]
            if g >= 4:
                wsc.append((sPE, o_done[g - 4]))
            def mksq(s, ps=ps, ubuf=ubuf, js=js):
                src = ps.rearrange("p (a c) -> p a c", c=512)[:, :, js:]
                dst = ubuf.rearrange("p (a c) -> p a c", c=512)[:, :, js:]
                return s.activation(dst, src, AF.Square, bias=c_rs, scale=c_rs)
            plan.op("scalar", mksq, waits=wsc, incs=[(sSc, 1)])
            sq_done[g] = sSc.n

            wv = [(sSc, sq_done[g]), (sIO, n_const)]
            ub = ubuf.rearrange("p (a c) -> p a c", c=512)
            if j >= 0:
                def mkd(v, ub=ub, js=js):
                    return v.scalar_tensor_tensor(
                        ub[:, :, js:js + 128], ub[:, :, js:js + 128], 0.5,
                        tri[:, None, :].broadcast_to([128, 2, 128]),
                        ALU.add, ALU.mult)
                plan.op("vector", mkd, waits=wv,
                        incs=[(sDV, 1)] if js + 128 >= 512 else ())
                if js + 128 < 512:
                    def mkt(v, ub=ub, js=js):
                        return v.tensor_scalar(ub[:, :, js + 128:],
                                               ub[:, :, js + 128:],
                                               0.5, None, ALU.add)
                    plan.op("vector", mkt, incs=[(sDV, 1)])
                aff_done[g] = (sDV, sDV.n)
            else:
                def mkf(v, ub=ub):
                    return v.tensor_scalar(ub[:, :, :], ub[:, :, :],
                                           0.5, None, ALU.add)
                plan.op("vector", mkf, waits=wv, incs=[(sDV, 1)])
                aff_done[g] = (sDV, sDV.n)

        def emit_o(idx, p=p, tiles=tiles):
            qc, kt, g = tiles[idx]
            gq = p * QC + qc
            j = kt - 4 * qc
            js = 128 * j if j >= 0 else 0
            ubuf = uab[g % 4]
            ob = oab[gq % 2]
            wpe = [aff_done[g]]
            if kt == 0 and gq >= 2:
                wpe.append(oevac_done[gq - 2])
            for ab in range(2):
                h = 2 * p + ab
                def mko(t, ab=ab, h=h, kt=kt, js=js, ubuf=ubuf, qc=qc, ob=ob):
                    lhs = vz[:, (kt * H + h) * 66:(kt * H + h) * 66 + 65]
                    rhs = ubuf[:, ab * 512 + js: ab * 512 + 512]
                    return t.matmul(ob[0:65, ab * 512 + js: ab * 512 + 512],
                                    lhs, rhs, start=(kt == 0),
                                    stop=(kt == 4 * qc + 3))
                plan.op("tensor", mko, waits=wpe if ab == 0 else (),
                        incs=[(sPE, 1)] if ab == 1 else ())
                wpe = ()
            o_done[g] = sPE.n
            if kt == 4 * qc + 3:
                emit_evac(p, qc)

        def emit_evac(p, qc):
            gq = p * QC + qc
            wsc = [(sPE, sPE.n)]
            if p >= 2 and qc == 0:
                wsc.append((sDV, div_done.get(p - 2, 0)))
            def mkoe(v, p=p, qc=qc, gq=gq):
                src = oab[gq % 2].rearrange("p (a c) -> p a c", c=512)[0:65, :, :]
                dst = ot_sb[p % 2].rearrange("p (a c) -> p a c", c=T)[
                    :, :, qc * 512:(qc + 1) * 512]
                return v.tensor_copy(dst, src)
            plan.op("vector", mkoe, waits=wsc, incs=[(sDV, 1)])
            oevac_done[gq] = (sDV, sDV.n)

        for i in range(ntl + LOOK):
            if i >= LOOK:
                emit_o(i - LOOK)
            if i < ntl:
                qc, kt, g = tiles[i]
                if qc == 1 and kt == 0 and p > 0:
                    plan_div_stage1(p - 1)
                if qc == 3 and kt == 0 and p > 0:
                    plan_div_stage2a(p - 1)
                if qc == 3 and kt == 8 and p > 0:
                    plan_div_stage2b(p - 1)
                emit_s(i)
                emit_sq_aff(i)

        wio = [oevac_done[p * QC + QC - 1]]
        if p > 0:
            wio.append((sDV, stage1_dv.get(p - 1, 0)))
        plan.op("sync",
                lambda sy, p=p: sy.dma_start(zg[:, :], ot_sb[p % 2][64:65, :]),
                waits=wio, incs=[(sIO, 16)])
        gather_done[p] = sIO.n
        if debug and p == 0:
            plan.op("sync", lambda sy: sy.dma_start(d_oso, ot_sb[0][:, :]),
                    incs=[(sIO, 16)])

    plan_div_stage1(NP - 1)
    plan_div_stage2a(NP - 1)
    plan_div_stage2b(NP - 1)

    ph2_psum.close()
    scope_a.close()
    scope_qkv.close()

    scope_w = contextlib.ExitStack()
    ew = scope_w.enter_context
    x1 = ew(nc.sbuf_tensor("x1", [128, NT * D], F32))
    scratch = ew(nc.sbuf_tensor("scratch", [128, D], F32))

    ph3_psum = contextlib.ExitStack()
    psy = [ph3_psum.enter_context(nc.psum_tensor(f"py{i}", [128, D], F32))
           for i in range(2)]

    w_all = [(sDV, div_done[NP - 1]), (sIO, obmove_done[NP - 1]), (sCa, n_wo)]

    if debug:
        plan.op("sync", lambda sy: sy.dma_start(d_oto, ot_fin[:, :]),
                waits=list(w_all), incs=[(sIO, 16)])

    r1_done = {}
    for t in range(NT):
        wio = []
        if t >= 2:
            wio.append((sDV, r1_done[t - 2]))
        plan.op("sync",
                lambda sy, t=t: sy.dma_start(xstream[t % 2][:, :],
                                             d_xtok[t * 128:(t + 1) * 128, :]),
                waits=wio, incs=[(sIO, 16)])
        n_xt = sIO.n

        ps = psy[t % 2]
        wpe = list(w_all) if t == 0 else []
        if t >= 2:
            wpe.append((sDV, r1_done[t - 2]))
        first = True
        for fi in range(NCH):
            for half in range(2):
                def mk(tt, ps=ps, fi=fi, half=half, t=t):
                    lhs = ot_fin[:, fi * T + t * 128: fi * T + t * 128 + 128]
                    rhs = woT[:, fi * D + half * 512: fi * D + half * 512 + 512]
                    return tt.matmul(ps[:, half * 512:(half + 1) * 512], lhs, rhs,
                                     start=(fi == 0), stop=(fi == NCH - 1))
                incs = [(sPE, 1)] if (fi == NCH - 1 and half == 1) else ()
                plan.op("tensor", mk, waits=wpe if first else (), incs=incs)
                first = False
        n_y = sPE.n

        def mkr1(v, t=t, ps=ps):
            return v.scalar_tensor_tensor(
                x1[:, t * D:(t + 1) * D], ps[:, :], 1.0, xstream[t % 2][:, :],
                ALU.mult, ALU.add, accum_out=S1[:, t:t + 1])
        plan.op("vector", mkr1, waits=[(sPE, n_y), (sIO, n_xt)], incs=[(sDV, 1)])
        r1_done[t] = sDV.n

        def mksq1(s, t=t):
            return s.activation(scratch[:, :], x1[:, t * D:(t + 1) * D], AF.Square,
                                accum_out=SQ1[:, t:t + 1])
        plan.op("scalar", mksq1, waits=[(sDV, r1_done[t])], incs=[(sSc, 1)])
        def mkmx(v, t=t):
            return v.reduce_max(MX[:, t:t + 1], x1[:, t * D:(t + 1) * D], AX.X)
        plan.op("vector", mkmx, waits=[(sDV, r1_done[t])])
        def mkmn(v, t=t):
            return v.tensor_reduce(MN[:, t:t + 1], x1[:, t * D:(t + 1) * D], AX.X,
                                   ALU.min)
        plan.op("vector", mkmn, incs=[(sDV, 1)])
    n_sq1 = sSc.n

    plan.op("vector", lambda v: v.tensor_scalar(M1, S1, INV1024, None, ALU.mult),
            waits=[(sDV, r1_done[NT - 1])], incs=[(sDV, 1)])
    plan.op("vector", lambda v: v.tensor_scalar(V1, SQ1, INV1024, None, ALU.mult),
            waits=[(sSc, n_sq1)], incs=[(sDV, 1)])
    plan.op("vector", lambda v: v.tensor_tensor(MSQ, M1, M1, ALU.mult),
            waits=[(sDV, sDV.n)], incs=[(sDV, 1)])
    plan.op("vector", lambda v: v.tensor_tensor(V1, V1, MSQ, ALU.subtract),
            waits=[(sDV, sDV.n)], incs=[(sDV, 1)])
    plan.op("scalar", lambda s: s.activation(V1, V1, AF.Sqrt, bias=c_eps),
            waits=[(sDV, sDV.n)], incs=[(sSc, 1)])
    plan.op("vector", lambda v: v.reciprocal(V1, V1),
            waits=[(sSc, sSc.n)], incs=[(sDV, 1)])
    n_rstd1 = sDV.n

    plan.op("vector", lambda v: v.tensor_tensor(MX, MX, M1, ALU.subtract),
            waits=[(sDV, n_rstd1)], incs=[(sDV, 1)])
    plan.op("vector", lambda v: v.tensor_tensor(MN, M1, MN, ALU.subtract),
            waits=[(sDV, sDV.n)], incs=[(sDV, 1)])
    plan.op("vector", lambda v: v.tensor_tensor(MX, MX, MN, ALU.max),
            waits=[(sDV, sDV.n)], incs=[(sDV, 1)])
    plan.op("vector", lambda v: v.tensor_tensor(AM, MX, V1, ALU.mult),
            waits=[(sDV, sDV.n)], incs=[(sDV, 1)])
    plan.op("vector",
            lambda v: v.tensor_scalar(XS, AM, 1e-6, INV127, ALU.max, ALU.mult),
            waits=[(sDV, sDV.n)], incs=[(sDV, 1)])
    plan.op("vector", lambda v: v.reciprocal(RQ, XS),
            waits=[(sDV, sDV.n)], incs=[(sDV, 1)])
    plan.op("vector", lambda v: v.tensor_tensor(S2, V1, RQ, ALU.mult),
            waits=[(sDV, sDV.n)], incs=[(sDV, 1)])
    plan.op("vector", lambda v: v.tensor_tensor(MM2, M1, S2, ALU.mult),
            waits=[(sDV, sDV.n)], incs=[(sDV, 1)])
    plan.op("vector",
            lambda v: v.tensor_scalar(MM2, MM2, -1.0, None, ALU.mult),
            waits=[(sDV, sDV.n)], incs=[(sDV, 1)])
    plan.op("vector", lambda v: v.tensor_tensor(NMR, M1, V1, ALU.mult),
            waits=[(sDV, sDV.n)], incs=[(sDV, 1)])
    plan.op("vector",
            lambda v: v.tensor_scalar(NMR, NMR, -1.0, None, ALU.mult),
            waits=[(sDV, sDV.n)], incs=[(sDV, 1)])
    n_qscale = sDV.n

    ph3_psum.close()

    scope_f = contextlib.ExitStack()
    ef = scope_f.enter_context
    wqf = ef(nc.sbuf_tensor("wqf", [128, NCH * D], F16))
    xqT = ef(nc.sbuf_tensor("xqT", [128, NCH * T], F16))
    xq = [ef(nc.sbuf_tensor(f"xq{i}", [128, D], F16)) for i in range(2)]
    t6 = [ef(nc.sbuf_tensor(f"t6{i}", [128, D], F32)) for i in range(2)]
    t7 = [ef(nc.sbuf_tensor(f"t7{i}", [128, D], F32)) for i in range(2)]
    t3 = [ef(nc.sbuf_tensor(f"t3{i}", [128, D], F32)) for i in range(2)]

    ph4_psum = contextlib.ExitStack()
    pst = [ph4_psum.enter_context(nc.psum_tensor(f"pt{i}", [128, 1024], F16))
           for i in range(2)]
    pacc = [ph4_psum.enter_context(nc.psum_tensor(f"pa{i}", [128, D], F32))
            for i in range(2)]

    for fi in range(NCH):
        plan.op("gpsimd",
                lambda g, fi=fi: g.dma_start(wqf[:, fi * D:(fi + 1) * D],
                                             d_wqq[fi * 128:(fi + 1) * 128, :]),
                waits=[(sDV, div_done[NP - 1]), (sIO, obmove_done[NP - 1])]
                if fi == 0 else (),
                incs=[(sCa, 16)])
    n_wq = sCa.n

    rnd1_done = {}
    rnd2_done = {}
    tr_done = {}
    trevac_done = {}
    x1b_done = {}
    for t in range(NT):
        wv = [(sDV, n_qscale)]
        if t >= 2:
            wv.append((sSc, rnd2_done[t - 2]))
        def mk1(v, t=t):
            return v.tensor_scalar(t6[t % 2][:, :], x1[:, t * D:(t + 1) * D],
                                   S2[:, t:t + 1], MM2[:, t:t + 1],
                                   ALU.mult, ALU.add)
        plan.op("vector", mk1, waits=wv, incs=[(sDV, 1)])
        rnd1_done[t] = sDV.n

        def mk2a(s, t=t):
            return s.activation(t7[t % 2][:, :], t6[t % 2][:, :], AF.Copy,
                                bias=MAGIC)
        plan.op("scalar", mk2a, waits=[(sDV, rnd1_done[t])], incs=[(sSc, 1)])
        wv = [(sSc, sSc.n)]
        if t >= 2:
            wv.append((sPE, tr_done[t - 2]))
        def mk2b(s, t=t):
            return s.activation(xq[t % 2][:, :], t7[t % 2][:, :], AF.Copy,
                                bias=-MAGIC)
        plan.op("scalar", mk2b, waits=wv, incs=[(sSc, 1)])
        rnd2_done[t] = sSc.n

        def mkxm(g, t=t):
            return g.tensor_scalar(x1[:, t * D:(t + 1) * D],
                                   x1[:, t * D:(t + 1) * D],
                                   V1[:, t:t + 1], NMR[:, t:t + 1],
                                   ALU.mult, ALU.add)
        plan.op("gpsimd", mkxm, waits=[(sDV, rnd1_done[t])], incs=[(sPo, 1)])
        n_xm = (sPo, sPo.n)
        if debug:
            plan.op("sync",
                    lambda sy, t=t: sy.dma_start(d_x1o[t * 128:(t + 1) * 128, :],
                                                 x1[:, t * D:(t + 1) * D]),
                    waits=[n_xm], incs=[(sIO, 16)])
        wx1b = [n_xm, (sIO, n_const)]
        if debug:
            wx1b.append((sIO, sIO.n))
        def mkx1b(g, t=t):
            return g.tensor_tensor(x1[:, t * D:(t + 1) * D],
                                   x1[:, t * D:(t + 1) * D], biasb[:, :], ALU.add)
        plan.op("gpsimd", mkx1b, waits=wx1b, incs=[(sPo, 1)])
        x1b_done[t] = sPo.n

        wpe = [(sSc, rnd2_done[t]), (sIO, n_const)]
        if t >= 2:
            wpe.append((sSc, trevac_done[t - 2]))
        for c in range(NCH):
            def mktr(tt, t=t, c=c):
                return tt.transpose(pst[t % 2][:, c * 128:(c + 1) * 128],
                                    xq[t % 2][:, c * 128:(c + 1) * 128],
                                    ident[:, :])
            plan.op("tensor", mktr, waits=wpe if c == 0 else (),
                    incs=[(sPE, 1)] if c == NCH - 1 else ())
            wpe = ()
        tr_done[t] = sPE.n
        def mktre(s, t=t):
            src = pst[t % 2].rearrange("p (c b) -> p c b", b=128)
            dst = xqT.rearrange("p (c u) -> p c u", u=T)[:, :, t * 128:(t + 1) * 128]
            return s.copy(dst, src)
        plan.op("scalar", mktre, waits=[(sPE, tr_done[t])], incs=[(sSc, 1)])
        trevac_done[t] = sSc.n

    if debug:
        plan.op("sync", lambda sy: sy.dma_start(d_xqo, xqT[:, :]),
                waits=[(sSc, trevac_done[NT - 1])], incs=[(sIO, 16)])

    t3_done = {}
    r2_done = {}
    for t in range(NT):
        ps = pacc[t % 2]
        wpe = [(sSc, trevac_done[t]), (sCa, n_wq)]
        if t >= 2:
            wpe.append((sDV, t3_done[t - 2]))
        first = True
        for fi in range(NCH):
            for half in range(2):
                def mk(tt, ps=ps, fi=fi, half=half, t=t):
                    lhs = xqT[:, fi * T + t * 128: fi * T + t * 128 + 128]
                    rhs = wqf[:, fi * D + half * 512: fi * D + half * 512 + 512]
                    return tt.matmul(ps[:, half * 512:(half + 1) * 512], lhs, rhs,
                                     start=(fi == 0), stop=(fi == NCH - 1))
                incs = [(sPE, 1)] if (fi == NCH - 1 and half == 1) else ()
                plan.op("tensor", mk, waits=wpe if first else (), incs=incs)
                first = False
        n_acc = sPE.n

        wv = [(sPE, n_acc), (sIO, n_const)]
        if t >= 2:
            wv.append((sDV, r2_done[t - 2]))
        def mkt3(v, t=t, ps=ps):
            return v.scalar_tensor_tensor(t3[t % 2][:, :], ps[:, :],
                                          XS[:, t:t + 1], wsb[:, :],
                                          ALU.mult, ALU.mult)
        plan.op("vector", mkt3, waits=wv, incs=[(sDV, 1)])
        t3_done[t] = sDV.n

        def mkr2(v, t=t):
            return v.scalar_tensor_tensor(
                x1[:, t * D:(t + 1) * D], t3[t % 2][:, :], 1.0,
                x1[:, t * D:(t + 1) * D], ALU.mult, ALU.add,
                accum_out=S1[:, t:t + 1])
        plan.op("vector", mkr2, waits=[(sDV, t3_done[t]), (sPo, x1b_done[t])],
                incs=[(sDV, 1)])
        r2_done[t] = sDV.n

        def mksq2(s, t=t):
            return s.activation(scratch[:, :], x1[:, t * D:(t + 1) * D], AF.Square,
                                accum_out=SQ1[:, t:t + 1])
        plan.op("scalar", mksq2, waits=[(sDV, r2_done[t])], incs=[(sSc, 1)])
    n_sq2 = sSc.n

    if debug:
        plan.op("sync", lambda sy: sy.dma_start(d_r2o, x1[:, :]),
                waits=[(sDV, r2_done[NT - 1])], incs=[(sIO, 16)])

    plan.op("vector", lambda v: v.tensor_scalar(M1, S1, INV1024, None, ALU.mult),
            waits=[(sDV, r2_done[NT - 1])], incs=[(sDV, 1)])
    plan.op("vector", lambda v: v.tensor_scalar(V1, SQ1, INV1024, None, ALU.mult),
            waits=[(sSc, n_sq2)], incs=[(sDV, 1)])
    plan.op("vector", lambda v: v.tensor_tensor(MSQ, M1, M1, ALU.mult),
            waits=[(sDV, sDV.n)], incs=[(sDV, 1)])
    plan.op("vector", lambda v: v.tensor_tensor(V1, V1, MSQ, ALU.subtract),
            waits=[(sDV, sDV.n)], incs=[(sDV, 1)])
    plan.op("scalar", lambda s: s.activation(V1, V1, AF.Sqrt, bias=c_eps),
            waits=[(sDV, sDV.n)], incs=[(sSc, 1)])
    plan.op("vector", lambda v: v.reciprocal(V1, V1),
            waits=[(sSc, sSc.n)], incs=[(sDV, 1)])
    n_rstd2 = sDV.n

    ostage = [xstream[0], xstream[1], t6[0], t6[1]]
    out_dma = {}
    for t in range(NT):
        wv = [(sDV, n_rstd2)]
        if t >= 4:
            wv.append((sIO, out_dma[t - 4]))
        def mka2(v, t=t):
            return v.tensor_scalar(ostage[t % 4][:, :], x1[:, t * D:(t + 1) * D],
                                   M1[:, t:t + 1], V1[:, t:t + 1],
                                   ALU.subtract, ALU.mult)
        plan.op("vector", mka2, waits=wv, incs=[(sDV, 1)])
        na = sDV.n
        plan.op("sync",
                lambda sy, t=t: sy.dma_start(d_out[t * 128:(t + 1) * 128, :],
                                             ostage[t % 4][:, :]),
                waits=[(sDV, na)], incs=[(sIO, 16)])
        out_dma[t] = sIO.n

    plan.op("sync", lambda sy: sy.nop(), waits=[(sIO, sIO.n)])

    if collect_only:
        ph4_psum.close()
        scope_f.close()
        scope_w.close()
        ctx.close()
        return nc, plan
    plan.emit()
    ph4_psum.close()
    scope_f.close()
    scope_w.close()
    ctx.close()
    return nc


_NC_CACHE = None


def kernel(L0_attn_hidden_states, L0_attn_Wq, L0_attn_Wk, L0_attn_Wv,
           L0_attn_Wo, L0_ffn_weight_q, L0_ffn_weight_scale, L0_ffn_bias):
    global _NC_CACHE
    hs = np.asarray(L0_attn_hidden_states, dtype=np.float32)
    assert hs.shape == (NCORES, T, D)

    wqt = np.ascontiguousarray(np.asarray(L0_attn_Wq, np.float32).T)
    wkt = np.ascontiguousarray(np.asarray(L0_attn_Wk, np.float32).T)
    wvt = np.ascontiguousarray(np.asarray(L0_attn_Wv, np.float32).T)
    wot = np.ascontiguousarray(np.asarray(L0_attn_Wo, np.float32).T)
    wqq = np.ascontiguousarray(np.asarray(L0_ffn_weight_q, np.int8).T)
    wsb = np.ascontiguousarray(np.broadcast_to(
        np.asarray(L0_ffn_weight_scale, np.float32)[None, :], (128, D)))
    biasb = np.ascontiguousarray(np.broadcast_to(
        np.asarray(L0_ffn_bias, np.float32)[None, :], (128, D)))
    tri = np.triu(np.ones((128, 128), np.float16))
    identm = np.eye(128, dtype=np.float16)
    ones2 = np.ones((2, 64), np.float16)

    if _NC_CACHE is None:
        _NC_CACHE = build_nc()
    nc = _NC_CACHE

    in_maps = []
    for i in range(NCORES):
        in_maps.append({
            "x_tok": np.ascontiguousarray(hs[i]),
            "xT": np.ascontiguousarray(hs[i].T),
            "wqt": wqt, "wkt": wkt, "wvt": wvt, "wot": wot,
            "wqq": wqq, "wsb": wsb, "biasb": biasb,
            "tri": tri, "ident": identm, "ones2": ones2,
        })
    res = run_bass_kernel_spmd(nc, in_maps, core_ids=list(range(NCORES)))
    out = np.stack([res.results[i]["out"] for i in range(NCORES)], axis=0)
    return out.astype(np.float32)


# version 26
# speedup vs baseline: 1.0272x; 1.0272x over previous
import contextlib
import numpy as np

import concourse.bass as bass
import concourse.mybir as mybir
from concourse.bass_utils import run_bass_kernel_spmd

F32 = mybir.dt.float32
F16 = mybir.dt.float16
I8 = mybir.dt.int8
AF = mybir.ActivationFunctionType
ALU = mybir.AluOpType
AX = mybir.AxisListType

NCORES = 8
T = 2048
D = 1024
H = 16
NT = T // 128
NCH = D // 128
NP = H // 2
QC = T // 512

RSQRT2 = 0.7071067811865476
QSCALE = 0.125
EPS_Z = 1e-6
EPS_LN = 1e-5
MAGIC = 12582912.0
INV1024 = 1.0 / 1024.0
INV127 = 1.0 / 127.0


class Sem:
    def __init__(self, handle, name=""):
        self.h = handle
        self.n = 0
        self.name = name


class Plan:

    ENGINES = ("sync", "gpsimd", "tensor", "scalar", "vector")

    def __init__(self, nc, last_wait):
        self.nc = nc
        self.streams = {e: [] for e in self.ENGINES}
        self.last_wait = last_wait
        self.waited = {}
        self.future_waits = {}

    def op(self, stream, builder, waits=(), incs=()):
        waits = list(waits)
        for s, k in incs:
            if k == 16:
                fw = self.future_waits.get(s.name)
                if fw:
                    import bisect
                    i = bisect.bisect_right(fw, s.n)
                    if i > 0 and fw[i - 1] > 0:
                        waits.append((s, fw[i - 1]))
        wl = []
        for s, v in waits:
            if v <= 0:
                continue
            self.waited.setdefault(s.name, set()).add(v)
            key = (stream, id(s))
            if self.last_wait.get(key, -1) >= v:
                continue
            self.last_wait[key] = v
            wl.append((s, v))
        for s, k in incs:
            s.n += k

        def thunk(eng, wl=wl, builder=builder, incs=tuple(incs)):
            for s, v in wl:
                eng.wait_ge(s.h, v)
            inst = builder(eng)
            for s, k in incs:
                inst = inst.then_inc(s.h, k)

        self.streams[stream].append(thunk)

    def emit(self):
        if not any(self.streams.values()):
            return
        with self.nc.Block() as block:
            for name in self.ENGINES:
                ths = self.streams[name]
                if not ths:
                    continue
                getattr(block, name)(lambda eng, ths=ths: [t(eng) for t in ths])


def build_nc(debug=False, _waits=None):
    if _waits is None:
        _waits = _collect_waits(debug)
    return _build_nc_inner(debug, _waits)


def _collect_waits(debug):
    nc, plan = _build_nc_inner(debug, {}, collect_only=True)
    return {k: sorted(v) for k, v in plan.waited.items()}


def _build_nc_inner(debug, _waits, collect_only=False):
    nc = bass.Bass("TRN2", target_bir_lowering=False,
                   dynamic_dma_scratch_size=8192)
    if debug:
        d_qto = nc.dram_tensor("qto", [128, NP * T], F16, kind="ExternalOutput").ap()
        d_kto = nc.dram_tensor("kto", [128, NP * T], F16, kind="ExternalOutput").ap()
        d_vzo = nc.dram_tensor("vzo", [128, NT * H * 66], F16, kind="ExternalOutput").ap()
        d_oto = nc.dram_tensor("oto", [128, NCH * T], F16, kind="ExternalOutput").ap()
        d_x1o = nc.dram_tensor("x1o", [T, D], F32, kind="ExternalOutput").ap()
        d_oso = nc.dram_tensor("oso", [65, 2 * T], F32, kind="ExternalOutput").ap()
        d_xqo = nc.dram_tensor("xqo", [128, NCH * T], F16, kind="ExternalOutput").ap()
        d_r2o = nc.dram_tensor("r2o", [128, NT * D], F32, kind="ExternalOutput").ap()

    d_xtok = nc.dram_tensor("x_tok", [T, D], F32, kind="ExternalInput").ap()
    d_xT = nc.dram_tensor("xT", [D, T], F32, kind="ExternalInput").ap()
    d_wqt = nc.dram_tensor("wqt", [D, D], F32, kind="ExternalInput").ap()
    d_wkt = nc.dram_tensor("wkt", [D, D], F32, kind="ExternalInput").ap()
    d_wvt = nc.dram_tensor("wvt", [D, D], F32, kind="ExternalInput").ap()
    d_wot = nc.dram_tensor("wot", [D, D], F32, kind="ExternalInput").ap()
    d_wqq = nc.dram_tensor("wqq", [D, D], I8, kind="ExternalInput").ap()
    d_wsb = nc.dram_tensor("wsb", [128, D], F32, kind="ExternalInput").ap()
    d_biasb = nc.dram_tensor("biasb", [128, D], F32, kind="ExternalInput").ap()
    d_tri = nc.dram_tensor("tri", [128, 128], F16, kind="ExternalInput").ap()
    d_ident = nc.dram_tensor("ident", [128, 128], F16, kind="ExternalInput").ap()
    d_ones2 = nc.dram_tensor("ones2", [2, 64], F16, kind="ExternalInput").ap()
    d_out = nc.dram_tensor("out", [T, D], F32, kind="ExternalOutput").ap()
    d_rzscr = nc.dram_tensor("rzscr", [2 * T], F32, kind="Internal").ap()

    ctx = contextlib.ExitStack()
    e = ctx.enter_context
    last_wait = {}

    sCa = Sem(e(nc.semaphore("sCa")), "sCa")
    sIO = Sem(e(nc.semaphore("sIO")), "sIO")
    sPE = Sem(e(nc.semaphore("sPE")), "sPE")
    sSc = Sem(e(nc.semaphore("sSc")), "sSc")
    sDV = Sem(e(nc.semaphore("sDV")), "sDV")
    sPo = Sem(e(nc.semaphore("sPo")), "sPo")

    R = dict(side="right")
    wsb = e(nc.sbuf_tensor("wsb_sb", [128, D], F32, **R))
    biasb = e(nc.sbuf_tensor("biasb_sb", [128, D], F32, **R))
    tri = e(nc.sbuf_tensor("tri_sb", [128, 128], F16, **R))
    ident = e(nc.sbuf_tensor("ident_sb", [128, 128], F16, **R))
    ones2 = e(nc.sbuf_tensor("ones2_sb", [2, 64], F16, **R))
    consts = e(nc.sbuf_tensor("consts", [128, 4], F32, **R))
    ot_fin = e(nc.sbuf_tensor("ot_fin", [128, NCH * T], F16, **R))
    ob_tmp = e(nc.sbuf_tensor("ob_tmp", [64, T], F16, **R))
    xstream = [e(nc.sbuf_tensor(f"xs{i}", [128, D], F32, **R)) for i in range(2)]
    stat = e(nc.sbuf_tensor("stat", [128, 13 * NT], F32, **R))
    woT = e(nc.sbuf_tensor("woT", [128, NCH * D], F16, **R))

    c_rs = consts[:, 0:1]
    c_eps = consts[:, 1:2]
    c_negm = consts[:, 2:3]
    S1 = stat[:, 0:NT]
    SQ1 = stat[:, NT:2 * NT]
    M1 = stat[:, 2 * NT:3 * NT]
    V1 = stat[:, 3 * NT:4 * NT]
    MSQ = stat[:, 4 * NT:5 * NT]
    AM = stat[:, 5 * NT:6 * NT]
    XS = stat[:, 6 * NT:7 * NT]
    RQ = stat[:, 7 * NT:8 * NT]
    MN = stat[:, 8 * NT:9 * NT]
    MX = stat[:, 9 * NT:10 * NT]
    S2 = stat[:, 10 * NT:11 * NT]
    MM2 = stat[:, 11 * NT:12 * NT]
    NMR = stat[:, 12 * NT:13 * NT]

    scope_qkv = contextlib.ExitStack()
    eq = scope_qkv.enter_context
    qT = eq(nc.sbuf_tensor("qT", [128, NP * T], F16))
    kT = eq(nc.sbuf_tensor("kT", [128, NP * T], F16))
    vz = eq(nc.sbuf_tensor("vz", [128, NT * H * 66], F16))
    vz_blocks = vz.rearrange("p (b c) -> p b c", c=66)

    scope_p = contextlib.ExitStack()
    ep = scope_p.enter_context
    xTsb = ep(nc.sbuf_tensor("xTsb", [128, NCH * T], F16))
    wbuf = ep(nc.sbuf_tensor("wbuf", [128, NCH * D], F16))

    plan = Plan(nc, last_wait)
    plan.future_waits = _waits

    for dst, src in ((wsb, d_wsb), (biasb, d_biasb), (tri, d_tri),
                     (ident, d_ident), (ones2, d_ones2)):
        plan.op("sync", lambda sy, dst=dst, src=src: sy.dma_start(dst[:, :], src),
                incs=[(sIO, 16)])
    n_const = sIO.n

    plan.op("vector", lambda v: v.memset(consts[:, 0:1], RSQRT2))
    plan.op("vector", lambda v: v.memset(consts[:, 1:2], EPS_LN))
    plan.op("vector", lambda v: v.memset(consts[:, 2:3], -MAGIC), incs=[(sDV, 1)])
    n_consts_dv = sDV.n

    dma_after = {}
    for fi in range(NCH):
        plan.op("gpsimd",
                lambda g, fi=fi: g.dma_start(xTsb[:, fi * T:(fi + 1) * T],
                                             d_xT[fi * 128:(fi + 1) * 128, :]),
                incs=[(sCa, 16)])
        plan.op("gpsimd",
                lambda g, fi=fi: g.dma_start(wbuf[:, fi * D:(fi + 1) * D],
                                             d_wqt[fi * 128:(fi + 1) * 128, :]),
                incs=[(sCa, 16)])
        dma_after[(0, fi)] = sCa.n

    ph1_psum = contextlib.ExitStack()
    psq = [ph1_psum.enter_context(nc.psum_tensor(f"pq{i}", [128, T], F32))
           for i in range(2)]

    evac_dv = {}
    mm_done = {}
    w_last_use = {}
    for proj in range(2):
        for c in range(NCH):
            ps = psq[c % 2]
            prev = None
            if c >= 2:
                prev = (proj, c - 2)
            elif proj > 0:
                prev = (proj - 1, NCH - 2 + c)
            for fi in range(NCH):
                waits = [(sCa, dma_after[(proj, fi)])]
                if prev in evac_dv:
                    waits.append((sDV, evac_dv[prev]))
                for qcc in range(QC):
                    def mk(t, ps=ps, fi=fi, qcc=qcc, c=c):
                        lhs = wbuf[:, fi * D + c * 128: fi * D + c * 128 + 128]
                        rhs = xTsb[:, fi * T + qcc * 512: fi * T + qcc * 512 + 512]
                        return t.matmul(ps[:, qcc * 512:(qcc + 1) * 512], lhs, rhs,
                                        start=(fi == 0), stop=(fi == NCH - 1))
                    incs = ()
                    if qcc == QC - 1 and (c == NCH - 1 or fi == NCH - 1):
                        incs = [(sPE, 1)]
                    plan.op("tensor", mk, waits=waits if qcc == 0 else (),
                            incs=incs)
                if c == NCH - 1:
                    w_last_use[(proj, fi)] = sPE.n
            mm_done[(proj, c)] = sPE.n
            dst = qT if proj == 0 else kT
            scale = QSCALE if proj == 0 else 1.0
            def mkev(v, ps=ps, dst=dst, scale=scale, c=c):
                return v.tensor_scalar(dst[:, c * T:(c + 1) * T], ps[:, :],
                                       scale, None, ALU.mult)
            plan.op("vector", mkev, waits=[(sPE, mm_done[(proj, c)])],
                    incs=[(sDV, 1)])
            evac_dv[(proj, c)] = sDV.n
        nxt = d_wkt if proj == 0 else d_wvt
        for fi in range(NCH):
            plan.op("gpsimd",
                    lambda g, fi=fi, nxt=nxt: g.dma_start(
                        wbuf[:, fi * D:(fi + 1) * D],
                        nxt[fi * 128:(fi + 1) * 128, :]),
                    waits=[(sPE, w_last_use[(proj, fi)])],
                    incs=[(sCa, 16)])
            dma_after[(proj + 1, fi)] = sCa.n
    n_qk_evac = sDV.n
    n_w3 = sCa.n

    ph1_psum.close()

    ph1b_psum = contextlib.ExitStack()
    psv = [ph1b_psum.enter_context(nc.psum_tensor(f"pv{i}", [128, D], F32))
           for i in range(2)]

    plan.op("vector", lambda v: v.memset(vz_blocks[:, :, 64:65], 1.0),
            incs=[(sDV, 1)])

    v_evac = {}
    for t in range(NT):
        ps = psv[t % 2]
        extra = []
        if t == 0:
            extra.append((sDV, n_qk_evac))
        if t >= 2:
            extra.append((sDV, v_evac[t - 2]))
        for fi in range(NCH):
            waits = [(sCa, dma_after[(2, fi)])] + extra
            extra = []
            for half in range(2):
                def mk(tt, ps=ps, fi=fi, half=half, t=t):
                    lhs = xTsb[:, fi * T + t * 128: fi * T + t * 128 + 128]
                    rhs = wbuf[:, fi * D + half * 512: fi * D + half * 512 + 512]
                    return tt.matmul(ps[:, half * 512:(half + 1) * 512], lhs, rhs,
                                     start=(fi == 0), stop=(fi == NCH - 1))
                incs = [(sPE, 1)] if (fi == NCH - 1 and half == 1) else ()
                plan.op("tensor", mk, waits=waits if half == 0 else (), incs=incs)
                waits = ()
        n_mm = sPE.n
        def mkev(v, ps=ps, t=t):
            src = ps.rearrange("p (h d) -> p h d", d=64)
            dst = vz_blocks[:, t * H:(t + 1) * H, 0:64]
            return v.tensor_copy(dst, src)
        plan.op("vector", mkev, waits=[(sPE, n_mm)], incs=[(sDV, 1)])
        v_evac[t] = sDV.n
    n_vz_done = sDV.n

    if debug:
        plan.op("sync", lambda sy: sy.dma_start(d_qto, qT[:, :]),
                waits=[(sDV, n_qk_evac)], incs=[(sIO, 16)])
        plan.op("sync", lambda sy: sy.dma_start(d_kto, kT[:, :]),
                waits=[(sDV, n_qk_evac)], incs=[(sIO, 16)])
        plan.op("sync", lambda sy: sy.dma_start(d_vzo, vz[:, :]),
                waits=[(sDV, n_vz_done)], incs=[(sIO, 16)])

    ph1b_psum.close()
    scope_p.close()

    scope_a = contextlib.ExitStack()
    ea = scope_a.enter_context
    uab = [ea(nc.sbuf_tensor(f"uab{i}", [128, 1024], F16)) for i in range(4)]
    ot_sb = [ea(nc.sbuf_tensor(f"otsb{i}", [65, 2 * T], F32)) for i in range(2)]
    zg = ea(nc.sbuf_tensor("zg", [32, 128], F32))
    rz2 = ea(nc.sbuf_tensor("rz2", [32, 128], F32))
    rzb = ea(nc.sbuf_tensor("rzb", [64, T], F32))

    ph2_psum = contextlib.ExitStack()
    psab = [ph2_psum.enter_context(nc.psum_tensor(f"ps{i}", [128, 1024], F32))
            for i in range(2)]
    oab = [ph2_psum.enter_context(nc.psum_tensor(f"oab{i}", [128, 1024], F32))
           for i in range(2)]

    for fi in range(NCH):
        plan.op("gpsimd",
                lambda g, fi=fi: g.dma_start(
                    woT[:, fi * D:(fi + 1) * D],
                    d_wot[fi * 128:(fi + 1) * 128, :]),
                incs=[(sCa, 16)])
    n_wo = sCa.n

    s_done = {}
    sq_done = {}
    o_done = {}
    oevac_done = {}
    div_done = {}
    bcast_done = {}
    bcastB_done = {}
    gather_done = {}
    scatter_done = {}
    obmove_done = {}
    stage1_dv = {}

    def plan_div_stage1(p):
        plan.op("vector",
                lambda v: v.tensor_scalar(zg[:, :], zg[:, :], EPS_Z, None, ALU.add),
                waits=[(sIO, gather_done[p])], incs=[(sDV, 1)])
        plan.op("vector", lambda v: v.reciprocal(rz2[:, :], zg[:, :]),
                waits=[(sDV, sDV.n)], incs=[(sDV, 1)])
        stage1_dv[p] = sDV.n
        w = [(sDV, stage1_dv[p])]
        if p > 0:
            w.append((sCa, bcastB_done.get(p - 1, 0)))
        plan.op("sync", lambda sy: sy.dma_start(d_rzscr, rz2[:, :]),
                waits=w, incs=[(sIO, 16)])
        scatter_done[p] = sIO.n
        wb = [(sIO, scatter_done[p])]
        if p > 0:
            wb.append((sDV, div_done.get(p - 1, 0)))
        def mkb(g):
            return g.dma_start(rzb[:, :],
                               d_rzscr[0:T][None, :].broadcast_to([64, T]))
        plan.op("gpsimd", mkb, waits=wb, incs=[(sCa, 16)])
        bcast_done[p] = sCa.n

    def plan_div_stage2a(p):
        c = p
        wv = [(sCa, bcast_done[p]), oevac_done[p * QC + QC - 1]]
        for qc in range(QC):
            def mka(v, p=p, qc=qc, c=c):
                return v.tensor_tensor(
                    ot_fin[0:64, c * T + qc * 512: c * T + qc * 512 + 512],
                    ot_sb[p % 2][0:64, qc * 512: qc * 512 + 512],
                    rzb[:, qc * 512: qc * 512 + 512], ALU.mult)
            plan.op("vector", mka, waits=wv, incs=[(sDV, 1)])
            wv = ()
        nda = sDV.n
        def mkbB(g):
            return g.dma_start(rzb[:, :],
                               d_rzscr[T:2 * T][None, :].broadcast_to([64, T]))
        plan.op("gpsimd", mkbB, waits=[(sDV, nda)], incs=[(sCa, 16)])
        bcastB_done[p] = sCa.n

    def plan_div_stage2b(p):
        wv = [(sCa, bcastB_done[p])]
        if p > 0:
            wv.append((sIO, obmove_done.get(p - 1, 0)))
        for qc in range(QC):
            def mkb2(v, p=p, qc=qc):
                return v.tensor_tensor(
                    ob_tmp[0:64, qc * 512: qc * 512 + 512],
                    ot_sb[p % 2][0:64, T + qc * 512: T + qc * 512 + 512],
                    rzb[:, qc * 512: qc * 512 + 512], ALU.mult)
            plan.op("vector", mkb2, waits=wv, incs=[(sDV, 1)])
            wv = ()
        div_done[p] = sDV.n
        wmv = [(sDV, div_done[p])]
        if p > 0:
            wmv.append((sIO, obmove_done.get(p - 1, 0)))
        c = p
        def mkmv(sy, c=c):
            return sy.dma_start(ot_fin[64:128, c * T:(c + 1) * T], ob_tmp[0:64, :])
        plan.op("sync", mkmv, waits=wmv, incs=[(sIO, 16)])
        obmove_done[p] = sIO.n

    LOOK = 3

    gk = 0
    aff_done = {}
    for p in range(NP):
        qT_c = qT[:, p * T:(p + 1) * T]
        kT_c = kT[:, p * T:(p + 1) * T]

        tiles = []
        for qc in range(QC):
            for kt in range(4 * qc + 4):
                tiles.append((qc, kt, gk))
                gk += 1
        ntl = len(tiles)
        base_gk = tiles[0][2]

        def emit_s(idx, p=p, tiles=tiles, qT_c=qT_c, kT_c=kT_c):
            qc, kt, g = tiles[idx]
            j = kt - 4 * qc
            js = 128 * j if j >= 0 else 0
            ps = psab[g % 2]
            wpe = []
            if g >= 2:
                wpe.append((sSc, sq_done[g - 2]))
            if g == 0:
                wpe.append((sDV, n_vz_done))
            for ab in range(2):
                r0 = ab * 64
                def mks(t, ab=ab, r0=r0, kt=kt, qc=qc, js=js, ps=ps,
                        qT_c=qT_c, kT_c=kT_c):
                    lhs = kT_c[r0:r0 + 64, kt * 128:(kt + 1) * 128]
                    rhs = qT_c[r0:r0 + 64, qc * 512 + js: qc * 512 + 512]
                    return t.matmul(ps[:, ab * 512 + js: ab * 512 + 512],
                                    lhs, rhs, start=True, stop=True,
                                    tile_position=(r0, 0))
                plan.op("tensor", mks, waits=wpe if ab == 0 else (),
                        incs=[(sPE, 1)] if ab == 1 else ())
                wpe = ()
            s_done[g] = sPE.n

        def emit_sq_aff(idx, tiles=tiles):
            qc, kt, g = tiles[idx]
            j = kt - 4 * qc
            js = 128 * j if j >= 0 else 0
            ps = psab[g % 2]
            ubuf = uab[g % 4]
            wsc = [(sPE, s_done[g]), (sDV, n_consts_dv)]
            if g >= 4:
                wsc.append((sPE, o_done[g - 4]))
            def mksq(s, ps=ps, ubuf=ubuf, js=js):
                src = ps.rearrange("p (a c) -> p a c", c=512)[:, :, js:]
                dst = ubuf.rearrange("p (a c) -> p a c", c=512)[:, :, js:]
                return s.activation(dst, src, AF.Square, bias=c_rs, scale=c_rs)
            plan.op("scalar", mksq, waits=wsc, incs=[(sSc, 1)])
            sq_done[g] = sSc.n

            wv = [(sSc, sq_done[g]), (sIO, n_const)]
            ub = ubuf.rearrange("p (a c) -> p a c", c=512)
            if j >= 0:
                def mkd(v, ub=ub, js=js):
                    return v.scalar_tensor_tensor(
                        ub[:, :, js:js + 128], ub[:, :, js:js + 128], 0.5,
                        tri[:, None, :].broadcast_to([128, 2, 128]),
                        ALU.add, ALU.mult)
                plan.op("vector", mkd, waits=wv,
                        incs=[(sDV, 1)] if js + 128 >= 512 else ())
                if js + 128 < 512:
                    def mkt(v, ub=ub, js=js):
                        return v.tensor_scalar(ub[:, :, js + 128:],
                                               ub[:, :, js + 128:],
                                               0.5, None, ALU.add)
                    plan.op("vector", mkt, incs=[(sDV, 1)])
                aff_done[g] = (sDV, sDV.n)
            else:
                def mkf(v, ub=ub):
                    return v.tensor_scalar(ub[:, :, :], ub[:, :, :],
                                           0.5, None, ALU.add)
                plan.op("vector", mkf, waits=wv, incs=[(sDV, 1)])
                aff_done[g] = (sDV, sDV.n)

        def emit_o(idx, p=p, tiles=tiles):
            qc, kt, g = tiles[idx]
            gq = p * QC + qc
            j = kt - 4 * qc
            js = 128 * j if j >= 0 else 0
            ubuf = uab[g % 4]
            ob = oab[gq % 2]
            wpe = [aff_done[g]]
            if kt == 0 and gq >= 2:
                wpe.append(oevac_done[gq - 2])
            for ab in range(2):
                h = 2 * p + ab
                def mko(t, ab=ab, h=h, kt=kt, js=js, ubuf=ubuf, qc=qc, ob=ob):
                    lhs = vz[:, (kt * H + h) * 66:(kt * H + h) * 66 + 65]
                    rhs = ubuf[:, ab * 512 + js: ab * 512 + 512]
                    return t.matmul(ob[0:65, ab * 512 + js: ab * 512 + 512],
                                    lhs, rhs, start=(kt == 0),
                                    stop=(kt == 4 * qc + 3))
                plan.op("tensor", mko, waits=wpe if ab == 0 else (),
                        incs=[(sPE, 1)] if ab == 1 else ())
                wpe = ()
            o_done[g] = sPE.n
            if kt == 4 * qc + 3:
                emit_evac(p, qc)

        def emit_evac(p, qc):
            gq = p * QC + qc
            wsc = [(sPE, sPE.n)]
            if p >= 2 and qc == 0:
                wsc.append((sDV, div_done.get(p - 2, 0)))
            def mkoe(v, p=p, qc=qc, gq=gq):
                src = oab[gq % 2].rearrange("p (a c) -> p a c", c=512)[0:65, :, :]
                dst = ot_sb[p % 2].rearrange("p (a c) -> p a c", c=T)[
                    :, :, qc * 512:(qc + 1) * 512]
                return v.tensor_copy(dst, src)
            plan.op("vector", mkoe, waits=wsc, incs=[(sDV, 1)])
            oevac_done[gq] = (sDV, sDV.n)

        for i in range(ntl + LOOK):
            if i >= LOOK:
                emit_o(i - LOOK)
            if i < ntl:
                qc, kt, g = tiles[i]
                if qc == 1 and kt == 0 and p > 0:
                    plan_div_stage1(p - 1)
                if qc == 3 and kt == 0 and p > 0:
                    plan_div_stage2a(p - 1)
                if qc == 3 and kt == 8 and p > 0:
                    plan_div_stage2b(p - 1)
                emit_s(i)
                emit_sq_aff(i)

        wio = [oevac_done[p * QC + QC - 1]]
        if p > 0:
            wio.append((sDV, stage1_dv.get(p - 1, 0)))
        plan.op("sync",
                lambda sy, p=p: sy.dma_start(zg[:, :], ot_sb[p % 2][64:65, :]),
                waits=wio, incs=[(sIO, 16)])
        gather_done[p] = sIO.n
        if debug and p == 0:
            plan.op("sync", lambda sy: sy.dma_start(d_oso, ot_sb[0][:, :]),
                    incs=[(sIO, 16)])

    plan_div_stage1(NP - 1)
    plan_div_stage2a(NP - 1)
    plan_div_stage2b(NP - 1)

    ph2_psum.close()
    scope_a.close()
    scope_qkv.close()

    scope_w = contextlib.ExitStack()
    ew = scope_w.enter_context
    x1 = ew(nc.sbuf_tensor("x1", [128, NT * D], F32))
    scratch = ew(nc.sbuf_tensor("scratch", [128, D], F32))

    ph3_psum = contextlib.ExitStack()
    psy = [ph3_psum.enter_context(nc.psum_tensor(f"py{i}", [128, D], F32))
           for i in range(2)]

    w_all = [(sDV, div_done[NP - 1]), (sIO, obmove_done[NP - 1]), (sCa, n_wo)]

    if debug:
        plan.op("sync", lambda sy: sy.dma_start(d_oto, ot_fin[:, :]),
                waits=list(w_all), incs=[(sIO, 16)])

    r1_done = {}
    for t in range(NT):
        wio = []
        if t >= 2:
            wio.append((sDV, r1_done[t - 2]))
        plan.op("sync",
                lambda sy, t=t: sy.dma_start(xstream[t % 2][:, :],
                                             d_xtok[t * 128:(t + 1) * 128, :]),
                waits=wio, incs=[(sIO, 16)])
        n_xt = sIO.n

        ps = psy[t % 2]
        wpe = list(w_all) if t == 0 else []
        if t >= 2:
            wpe.append((sDV, r1_done[t - 2]))
        first = True
        for fi in range(NCH):
            for half in range(2):
                def mk(tt, ps=ps, fi=fi, half=half, t=t):
                    lhs = ot_fin[:, fi * T + t * 128: fi * T + t * 128 + 128]
                    rhs = woT[:, fi * D + half * 512: fi * D + half * 512 + 512]
                    return tt.matmul(ps[:, half * 512:(half + 1) * 512], lhs, rhs,
                                     start=(fi == 0), stop=(fi == NCH - 1))
                incs = [(sPE, 1)] if (fi == NCH - 1 and half == 1) else ()
                plan.op("tensor", mk, waits=wpe if first else (), incs=incs)
                first = False
        n_y = sPE.n

        def mkr1(v, t=t, ps=ps):
            return v.scalar_tensor_tensor(
                x1[:, t * D:(t + 1) * D], ps[:, :], 1.0, xstream[t % 2][:, :],
                ALU.mult, ALU.add, accum_out=S1[:, t:t + 1])
        plan.op("vector", mkr1, waits=[(sPE, n_y), (sIO, n_xt)], incs=[(sDV, 1)])
        r1_done[t] = sDV.n

        def mksq1(s, t=t):
            return s.activation(scratch[:, :], x1[:, t * D:(t + 1) * D], AF.Square,
                                accum_out=SQ1[:, t:t + 1])
        plan.op("scalar", mksq1, waits=[(sDV, r1_done[t]), (sSc, sSc.n)],
                incs=[(sSc, 1)])
        def mkmx(v, t=t):
            return v.reduce_max(MX[:, t:t + 1], x1[:, t * D:(t + 1) * D], AX.X)
        plan.op("vector", mkmx, waits=[(sDV, r1_done[t])])
        def mkmn(v, t=t):
            return v.tensor_reduce(MN[:, t:t + 1], x1[:, t * D:(t + 1) * D], AX.X,
                                   ALU.min)
        plan.op("vector", mkmn, incs=[(sDV, 1)])
    n_sq1 = sSc.n

    plan.op("vector", lambda v: v.tensor_scalar(M1, S1, INV1024, None, ALU.mult),
            waits=[(sDV, r1_done[NT - 1])], incs=[(sDV, 1)])
    plan.op("vector", lambda v: v.tensor_scalar(V1, SQ1, INV1024, None, ALU.mult),
            waits=[(sSc, n_sq1)], incs=[(sDV, 1)])
    plan.op("vector", lambda v: v.tensor_tensor(MSQ, M1, M1, ALU.mult),
            waits=[(sDV, sDV.n)], incs=[(sDV, 1)])
    plan.op("vector", lambda v: v.tensor_tensor(V1, V1, MSQ, ALU.subtract),
            waits=[(sDV, sDV.n)], incs=[(sDV, 1)])
    plan.op("scalar", lambda s: s.activation(V1, V1, AF.Sqrt, bias=c_eps),
            waits=[(sDV, sDV.n)], incs=[(sSc, 1)])
    plan.op("vector", lambda v: v.reciprocal(V1, V1),
            waits=[(sSc, sSc.n)], incs=[(sDV, 1)])
    n_rstd1 = sDV.n

    plan.op("vector", lambda v: v.tensor_tensor(MX, MX, M1, ALU.subtract),
            waits=[(sDV, n_rstd1)], incs=[(sDV, 1)])
    plan.op("vector", lambda v: v.tensor_tensor(MN, M1, MN, ALU.subtract),
            waits=[(sDV, sDV.n)], incs=[(sDV, 1)])
    plan.op("vector", lambda v: v.tensor_tensor(MX, MX, MN, ALU.max),
            waits=[(sDV, sDV.n)], incs=[(sDV, 1)])
    plan.op("vector", lambda v: v.tensor_tensor(AM, MX, V1, ALU.mult),
            waits=[(sDV, sDV.n)], incs=[(sDV, 1)])
    plan.op("vector",
            lambda v: v.tensor_scalar(XS, AM, 1e-6, INV127, ALU.max, ALU.mult),
            waits=[(sDV, sDV.n)], incs=[(sDV, 1)])
    plan.op("vector", lambda v: v.reciprocal(RQ, XS),
            waits=[(sDV, sDV.n)], incs=[(sDV, 1)])
    plan.op("vector", lambda v: v.tensor_tensor(S2, V1, RQ, ALU.mult),
            waits=[(sDV, sDV.n)], incs=[(sDV, 1)])
    plan.op("vector", lambda v: v.tensor_tensor(MM2, M1, S2, ALU.mult),
            waits=[(sDV, sDV.n)], incs=[(sDV, 1)])
    plan.op("vector",
            lambda v: v.tensor_scalar(MM2, MM2, -1.0, None, ALU.mult),
            waits=[(sDV, sDV.n)], incs=[(sDV, 1)])
    plan.op("vector", lambda v: v.tensor_tensor(NMR, M1, V1, ALU.mult),
            waits=[(sDV, sDV.n)], incs=[(sDV, 1)])
    plan.op("vector",
            lambda v: v.tensor_scalar(NMR, NMR, -1.0, None, ALU.mult),
            waits=[(sDV, sDV.n)], incs=[(sDV, 1)])
    n_qscale = sDV.n

    ph3_psum.close()

    scope_f = contextlib.ExitStack()
    ef = scope_f.enter_context
    wqf = ef(nc.sbuf_tensor("wqf", [128, NCH * D], F16))
    xqT = ef(nc.sbuf_tensor("xqT", [128, NCH * T], F16))
    xq = [ef(nc.sbuf_tensor(f"xq{i}", [128, D], F16)) for i in range(2)]
    t6 = [ef(nc.sbuf_tensor(f"t6{i}", [128, D], F32)) for i in range(2)]
    t7 = [ef(nc.sbuf_tensor(f"t7{i}", [128, D], F32)) for i in range(2)]
    t3 = [ef(nc.sbuf_tensor(f"t3{i}", [128, D], F32)) for i in range(2)]

    ph4_psum = contextlib.ExitStack()
    pst = [ph4_psum.enter_context(nc.psum_tensor(f"pt{i}", [128, 1024], F16))
           for i in range(2)]
    pacc = [ph4_psum.enter_context(nc.psum_tensor(f"pa{i}", [128, D], F32))
            for i in range(2)]

    for fi in range(NCH):
        plan.op("gpsimd",
                lambda g, fi=fi: g.dma_start(wqf[:, fi * D:(fi + 1) * D],
                                             d_wqq[fi * 128:(fi + 1) * 128, :]),
                waits=[(sDV, div_done[NP - 1]), (sIO, obmove_done[NP - 1])]
                if fi == 0 else (),
                incs=[(sCa, 16)])
    n_wq = sCa.n

    rnd1_done = {}
    rnd2_done = {}
    tr_done = {}
    trevac_done = {}
    x1b_done = {}
    for t in range(NT):
        wv = [(sDV, n_qscale)]
        if t >= 2:
            wv.append((sSc, rnd2_done[t - 2]))
        def mk1(v, t=t):
            return v.tensor_scalar(t6[t % 2][:, :], x1[:, t * D:(t + 1) * D],
                                   S2[:, t:t + 1], MM2[:, t:t + 1],
                                   ALU.mult, ALU.add)
        plan.op("vector", mk1, waits=wv, incs=[(sDV, 1)])
        rnd1_done[t] = sDV.n

        def mk2a(s, t=t):
            return s.activation(t7[t % 2][:, :], t6[t % 2][:, :], AF.Copy,
                                bias=MAGIC)
        plan.op("scalar", mk2a, waits=[(sDV, rnd1_done[t])], incs=[(sSc, 1)])
        wv = [(sSc, sSc.n)]
        if t >= 2:
            wv.append((sPE, tr_done[t - 2]))
        def mk2b(s, t=t):
            return s.activation(xq[t % 2][:, :], t7[t % 2][:, :], AF.Copy,
                                bias=-MAGIC)
        plan.op("scalar", mk2b, waits=wv, incs=[(sSc, 1)])
        rnd2_done[t] = sSc.n

        def mkxm(g, t=t):
            return g.tensor_scalar(x1[:, t * D:(t + 1) * D],
                                   x1[:, t * D:(t + 1) * D],
                                   V1[:, t:t + 1], NMR[:, t:t + 1],
                                   ALU.mult, ALU.add)
        plan.op("gpsimd", mkxm, waits=[(sDV, rnd1_done[t])], incs=[(sPo, 1)])
        n_xm = (sPo, sPo.n)
        if debug:
            plan.op("sync",
                    lambda sy, t=t: sy.dma_start(d_x1o[t * 128:(t + 1) * 128, :],
                                                 x1[:, t * D:(t + 1) * D]),
                    waits=[n_xm], incs=[(sIO, 16)])
        wx1b = [n_xm, (sIO, n_const)]
        if debug:
            wx1b.append((sIO, sIO.n))
        def mkx1b(g, t=t):
            return g.tensor_tensor(x1[:, t * D:(t + 1) * D],
                                   x1[:, t * D:(t + 1) * D], biasb[:, :], ALU.add)
        plan.op("gpsimd", mkx1b, waits=wx1b, incs=[(sPo, 1)])
        x1b_done[t] = sPo.n

        wpe = [(sSc, rnd2_done[t]), (sIO, n_const)]
        if t >= 2:
            wpe.append((sSc, trevac_done[t - 2]))
        for c in range(NCH):
            def mktr(tt, t=t, c=c):
                return tt.transpose(pst[t % 2][:, c * 128:(c + 1) * 128],
                                    xq[t % 2][:, c * 128:(c + 1) * 128],
                                    ident[:, :])
            plan.op("tensor", mktr, waits=wpe if c == 0 else (),
                    incs=[(sPE, 1)] if c == NCH - 1 else ())
            wpe = ()
        tr_done[t] = sPE.n
        def mktre(s, t=t):
            src = pst[t % 2].rearrange("p (c b) -> p c b", b=128)
            dst = xqT.rearrange("p (c u) -> p c u", u=T)[:, :, t * 128:(t + 1) * 128]
            return s.copy(dst, src)
        plan.op("scalar", mktre, waits=[(sPE, tr_done[t])], incs=[(sSc, 1)])
        trevac_done[t] = sSc.n

    if debug:
        plan.op("sync", lambda sy: sy.dma_start(d_xqo, xqT[:, :]),
                waits=[(sSc, trevac_done[NT - 1])], incs=[(sIO, 16)])

    t3_done = {}
    r2_done = {}
    for t in range(NT):
        ps = pacc[t % 2]
        wpe = [(sSc, trevac_done[t]), (sCa, n_wq)]
        if t >= 2:
            wpe.append((sDV, t3_done[t - 2]))
        first = True
        for fi in range(NCH):
            for half in range(2):
                def mk(tt, ps=ps, fi=fi, half=half, t=t):
                    lhs = xqT[:, fi * T + t * 128: fi * T + t * 128 + 128]
                    rhs = wqf[:, fi * D + half * 512: fi * D + half * 512 + 512]
                    return tt.matmul(ps[:, half * 512:(half + 1) * 512], lhs, rhs,
                                     start=(fi == 0), stop=(fi == NCH - 1))
                incs = [(sPE, 1)] if (fi == NCH - 1 and half == 1) else ()
                plan.op("tensor", mk, waits=wpe if first else (), incs=incs)
                first = False
        n_acc = sPE.n

        wv = [(sPE, n_acc), (sIO, n_const)]
        if t >= 2:
            wv.append((sDV, r2_done[t - 2]))
        def mkt3(v, t=t, ps=ps):
            return v.scalar_tensor_tensor(t3[t % 2][:, :], ps[:, :],
                                          XS[:, t:t + 1], wsb[:, :],
                                          ALU.mult, ALU.mult)
        plan.op("vector", mkt3, waits=wv, incs=[(sDV, 1)])
        t3_done[t] = sDV.n

        def mkr2(v, t=t):
            return v.scalar_tensor_tensor(
                x1[:, t * D:(t + 1) * D], t3[t % 2][:, :], 1.0,
                x1[:, t * D:(t + 1) * D], ALU.mult, ALU.add,
                accum_out=S1[:, t:t + 1])
        plan.op("vector", mkr2, waits=[(sDV, t3_done[t]), (sPo, x1b_done[t])],
                incs=[(sDV, 1)])
        r2_done[t] = sDV.n

        def mksq2(s, t=t):
            return s.activation(scratch[:, :], x1[:, t * D:(t + 1) * D], AF.Square,
                                accum_out=SQ1[:, t:t + 1])
        plan.op("scalar", mksq2, waits=[(sDV, r2_done[t]), (sSc, sSc.n)],
                incs=[(sSc, 1)])
    n_sq2 = sSc.n

    if debug:
        plan.op("sync", lambda sy: sy.dma_start(d_r2o, x1[:, :]),
                waits=[(sDV, r2_done[NT - 1])], incs=[(sIO, 16)])

    plan.op("vector", lambda v: v.tensor_scalar(M1, S1, INV1024, None, ALU.mult),
            waits=[(sDV, r2_done[NT - 1])], incs=[(sDV, 1)])
    plan.op("vector", lambda v: v.tensor_scalar(V1, SQ1, INV1024, None, ALU.mult),
            waits=[(sSc, n_sq2)], incs=[(sDV, 1)])
    plan.op("vector", lambda v: v.tensor_tensor(MSQ, M1, M1, ALU.mult),
            waits=[(sDV, sDV.n)], incs=[(sDV, 1)])
    plan.op("vector", lambda v: v.tensor_tensor(V1, V1, MSQ, ALU.subtract),
            waits=[(sDV, sDV.n)], incs=[(sDV, 1)])
    plan.op("scalar", lambda s: s.activation(V1, V1, AF.Sqrt, bias=c_eps),
            waits=[(sDV, sDV.n)], incs=[(sSc, 1)])
    plan.op("vector", lambda v: v.reciprocal(V1, V1),
            waits=[(sSc, sSc.n)], incs=[(sDV, 1)])
    n_rstd2 = sDV.n

    ostage = [xstream[0], xstream[1], t6[0], t6[1]]
    out_dma = {}
    for t in range(NT):
        wv = [(sDV, n_rstd2)]
        if t >= 4:
            wv.append((sIO, out_dma[t - 4]))
        def mka2(v, t=t):
            return v.tensor_scalar(ostage[t % 4][:, :], x1[:, t * D:(t + 1) * D],
                                   M1[:, t:t + 1], V1[:, t:t + 1],
                                   ALU.subtract, ALU.mult)
        plan.op("vector", mka2, waits=wv, incs=[(sDV, 1)])
        na = sDV.n
        plan.op("sync",
                lambda sy, t=t: sy.dma_start(d_out[t * 128:(t + 1) * 128, :],
                                             ostage[t % 4][:, :]),
                waits=[(sDV, na)], incs=[(sIO, 16)])
        out_dma[t] = sIO.n

    plan.op("sync", lambda sy: sy.nop(), waits=[(sIO, sIO.n)])

    if collect_only:
        ph4_psum.close()
        scope_f.close()
        scope_w.close()
        ctx.close()
        return nc, plan
    plan.emit()
    ph4_psum.close()
    scope_f.close()
    scope_w.close()
    ctx.close()
    return nc


_NC_CACHE = None


def kernel(L0_attn_hidden_states, L0_attn_Wq, L0_attn_Wk, L0_attn_Wv,
           L0_attn_Wo, L0_ffn_weight_q, L0_ffn_weight_scale, L0_ffn_bias):
    global _NC_CACHE
    hs = np.asarray(L0_attn_hidden_states, dtype=np.float32)
    assert hs.shape == (NCORES, T, D)

    wqt = np.ascontiguousarray(np.asarray(L0_attn_Wq, np.float32).T)
    wkt = np.ascontiguousarray(np.asarray(L0_attn_Wk, np.float32).T)
    wvt = np.ascontiguousarray(np.asarray(L0_attn_Wv, np.float32).T)
    wot = np.ascontiguousarray(np.asarray(L0_attn_Wo, np.float32).T)
    wqq = np.ascontiguousarray(np.asarray(L0_ffn_weight_q, np.int8).T)
    wsb = np.ascontiguousarray(np.broadcast_to(
        np.asarray(L0_ffn_weight_scale, np.float32)[None, :], (128, D)))
    biasb = np.ascontiguousarray(np.broadcast_to(
        np.asarray(L0_ffn_bias, np.float32)[None, :], (128, D)))
    tri = np.triu(np.ones((128, 128), np.float16))
    identm = np.eye(128, dtype=np.float16)
    ones2 = np.ones((2, 64), np.float16)

    if _NC_CACHE is None:
        _NC_CACHE = build_nc()
    nc = _NC_CACHE

    in_maps = []
    for i in range(NCORES):
        in_maps.append({
            "x_tok": np.ascontiguousarray(hs[i]),
            "xT": np.ascontiguousarray(hs[i].T),
            "wqt": wqt, "wkt": wkt, "wvt": wvt, "wot": wot,
            "wqq": wqq, "wsb": wsb, "biasb": biasb,
            "tri": tri, "ident": identm, "ones2": ones2,
        })
    res = run_bass_kernel_spmd(nc, in_maps, core_ids=list(range(NCORES)))
    out = np.stack([res.results[i]["out"] for i in range(NCORES)], axis=0)
    return out.astype(np.float32)
